# version 03
# baseline (speedup 1.0000x reference)
import numpy as np
import ml_dtypes
import concourse.bass as bass
import concourse.mybir as mybir
from concourse.bass_utils import run_bass_kernel_spmd
from concourse.bass import IndirectOffsetOnAxis

F32 = mybir.dt.float32
BF16 = mybir.dt.bfloat16
U32 = mybir.dt.uint32
AF = mybir.ActivationFunctionType
ALU = mybir.AluOpType
BF = ml_dtypes.bfloat16

NCORES = 8
D = 1024
L = 16384
T_SH = 2048
HALO = 128
T = T_SH + HALO
NT = T // 128
EPS_LN = 1e-5


def ceil_div(a, b):
    return (a + b - 1) // b


def n_chunks(total, step=512):
    out = []
    s = 0
    while s < total:
        out.append((s, min(step, total - s)))
        s += step
    return out


class B:

    ENGINES = ("sync", "gpsimd", "vector", "scalar", "tensor")

    def __init__(self, nc):
        self.nc = nc
        self.streams = {e: [] for e in self.ENGINES}
        self.counts = {}
        self.sh = {}

    def _variants(self, name):
        return [k for k in self.counts if k.rsplit("@", 1)[0] == name]

    def val(self, name):
        return sum(self.counts[k] for k in self._variants(name))

    def _bump(self, key, k):
        self.counts[key] = self.counts.get(key, 0) + k
        return self.counts[key]

    def raw(self, engine, fn):
        self.streams[engine].append(fn)

    def wait(self, engine, name, vv=None):
        if vv is not None:
            if "@" in name:
                ks = [name]
            else:
                ks = self._variants(name)
            assert len(ks) == 1, f"explicit wait on multi-producer sem {name}: {ks}"
            if vv <= 0:
                return
            self.streams[engine].append(lambda e, k=ks[0], v=vv: e.wait_ge(self.sh[k], v))
            return
        for k in self._variants(name):
            v = self.counts[k]
            if v > 0:
                self.streams[engine].append(
                    lambda e, k=k, v=v: e.wait_ge(self.sh[k], v))

    def op(self, engine, fn, incs=()):
        keys = []
        for n, k in incs:
            key = f"{n}@{engine}"
            self._bump(key, k)
            keys.append((key, k))
        def f(e):
            ins = fn(e)
            for key, k in keys:
                if k > 0:
                    ins = ins.then_inc(self.sh[key], k)
        self.streams[engine].append(f)

    def finalize(self, stack):
        nc = self.nc
        for name in self.counts:
            self.sh[name] = stack.enter_context(nc.semaphore(name))
        block = stack.enter_context(nc.Block())
        for eng_name in self.ENGINES:
            fns = self.streams[eng_name]
            if not fns:
                continue
            def make(fns):
                def runner(eng):
                    for f in fns:
                        f(eng)
                return runner
            getattr(block, eng_name)(make(fns))



def emit_ln_tile(b, xi, xn_out, stats6, stats2, rcol, tag, epscol=None):
    s = tag
    b.op("vector", lambda v: v.bn_stats(stats6[:, 0:6], xi[:, 0:512]), [(s, 1)])
    b.op("vector", lambda v: v.bn_stats(stats6[:, 6:12], xi[:, 512:1024]), [(s, 1)])
    b.wait("vector", s)
    b.op("vector", lambda v: v.bn_aggr(stats2[:, :], stats6[:, :]), [(s, 1)])
    b.wait("scalar", s)
    b.op("scalar", lambda sc: sc.activation(rcol[:, 0:1], stats2[:, 1:2], AF.Ln,
                                            bias=epscol), [(s, 1)])
    b.wait("scalar", s)
    b.op("scalar", lambda sc: sc.activation(rcol[:, 1:2], rcol[:, 0:1], AF.Exp,
                                            scale=-0.5), [(s, 1)])
    b.wait("vector", s)
    b.op("vector", lambda v: v.tensor_scalar(xn_out, xi, stats2[:, 0:1], rcol[:, 1:2],
                                             op0=ALU.subtract, op1=ALU.mult), [(s, 1)])



def build_launch1():
    import contextlib
    nc = bass.Bass("TRN2", target_bir_lowering=False, debug=False)
    x = nc.dram_tensor("x", [T, D], F32, kind="ExternalInput").ap()
    wqkv = nc.dram_tensor("wqkv", [D, 3 * D], F32, kind="ExternalInput").ap()
    bqkv = nc.dram_tensor("bqkv", [3 * D], F32, kind="ExternalInput").ap()
    xn_o = nc.dram_tensor("xn_o", [T, D], BF16, kind="ExternalOutput").ap()
    qT_o = nc.dram_tensor("qT_o", [D, T], BF16, kind="ExternalOutput").ap()
    k_o = nc.dram_tensor("k_o", [T, D], BF16, kind="ExternalOutput").ap()
    v_o = nc.dram_tensor("v_o", [T, D], BF16, kind="ExternalOutput").ap()

    xn_ov = xn_o.rearrange("(t p) d -> p t d", p=128)
    k_ov = k_o.rearrange("(t p) d -> p t d", p=128)
    v_ov = v_o.rearrange("(t p) d -> p t d", p=128)
    qT_ov = qT_o.rearrange("(c p) t -> p c t", p=128)

    stack = contextlib.ExitStack()
    sb = lambda *a: stack.enter_context(nc.sbuf_tensor(*a))
    ps = lambda *a: stack.enter_context(nc.psum_tensor(*a))

    xt = sb("xt", [128, 2, D], F32)
    xn = sb("xn", [128, NT, D], BF16)
    xnT = sb("xnT", [128, 8, T], BF16)
    wq = sb("wq", [128, 8, 3 * D], BF16)
    bq_pp = sb("bq_pp", [128, 8], F32)
    bkv_bc = sb("bkv_bc", [128, 2 * D], F32)
    ident = sb("ident", [128, 128], BF16)
    st6 = sb("st6", [128, 12], F32)
    epscol = sb("epscol", [128, 1], F32)
    st2 = sb("st2", [128, 2], F32)
    rcol = sb("rcol", [128, 2], F32)
    qstage = sb("qstage", [128, 2, 512], BF16)
    kvstage = sb("kvstage", [128, 2, 512], BF16)
    ps_t = ps("ps_t", [128, 128], BF16)
    ps_a = ps("ps_a", [128, 512], F32)
    ps_b = ps("ps_b", [128, 512], F32)
    ps_c = ps("ps_c", [128, 512], F32)
    ps_d = ps("ps_d", [128, 512], F32)

    b = B(nc)

    b.op("vector", lambda v: v.memset(epscol[:, :], EPS_LN), [("ln", 0), ("eps", 1)])
    b.wait("scalar", "eps")
    b.op("gpsimd", lambda g: g.memset(ident[:, :], 1.0), [("gp", 1)])
    b.wait("gpsimd", "gp")
    b.op("gpsimd", lambda g: g.affine_select(ident[:, :], ident[:, :],
                                             pattern=[[-1, 128]],
                                             compare_op=ALU.is_equal, fill=0.0,
                                             base=0, channel_multiplier=1),
         [("ident", 1)])
    for c in range(8):
        b.op("gpsimd", lambda g, c=c: g.dma_start(out=wq[:, c, :],
                                                  in_=wqkv[128 * c:128 * (c + 1), :]),
             [("wl", 16)])
    for c in range(8):
        b.op("sync", lambda sy, c=c: sy.dma_start(
            out=bq_pp[:, c:c + 1],
            in_=bqkv[128 * c:128 * (c + 1)].rearrange("(p o) -> p o", o=1)),
            [("wl", 16)])
    b.op("sync", lambda sy: sy.dma_start(
        out=bkv_bc[:, :],
        in_=bqkv[D:3 * D].rearrange("(o d) -> o d", o=1).to_broadcast([128, 2 * D])),
        [("wl", 16)])

    for i in range(NT):
        if i >= 2:
            b.wait("sync", "ln@vector", 4 * (i - 1))
        b.op("sync", lambda sy, i=i: sy.dma_start(out=xt[:, i % 2, :],
                                                  in_=x[128 * i:128 * (i + 1), :]),
             [(f"xl{i % 2}", 16)])
        b.wait("vector", f"xl{i % 2}", 16 * (i // 2 + 1))
        emit_ln_tile(b, xt[:, i % 2, :], xn[:, i, :], st6, st2, rcol, "ln",
                     epscol=epscol[:, 0:1])
        b.wait("sync", "ln")
        b.op("sync", lambda sy, i=i: sy.dma_start(out=xn_ov[:, i, :], in_=xn[:, i, :]),
             [("xnst", 16)])
        b.wait("tensor", "ident")
        b.wait("tensor", "ln")
        for c in range(8):
            b.op("tensor", lambda t, i=i, c=c: t.transpose(
                ps_t[:, :], xn[:, i, 128 * c:128 * (c + 1)], ident[:, :]), [("tp", 1)])
            b.wait("vector", "tp")
            b.op("vector", lambda v, i=i, c=c: v.tensor_copy(
                xnT[:, c, 128 * i:128 * (i + 1)], ps_t[:, :]), [("tpc", 1)])
            b.wait("tensor", "tpc")

    b.wait("tensor", "wl")
    b.wait("vector", "wl")
    pso = [ps_a, ps_b]
    nq = 0
    for fc in range(8):
        for (s0, ncol) in n_chunks(T):
            p = pso[nq % 2]
            if nq >= 2:
                b.wait("tensor", "qev", nq - 1)
            for cd in range(8):
                b.op("tensor", lambda t, p=p, cd=cd, fc=fc, s0=s0, ncol=ncol: t.matmul(
                    p[:, 0:ncol], wq[:, cd, 128 * fc:128 * (fc + 1)],
                    xnT[:, cd, s0:s0 + ncol], start=(cd == 0), stop=(cd == 7)),
                    [("qmm", 1)] if cd == 7 else [])
            b.wait("vector", "qmm")
            if nq >= 2:
                b.wait("vector", f"qst{nq % 2}", 16 * (nq // 2))
            b.op("vector", lambda v, p=p, fc=fc, ncol=ncol, nq=nq: v.tensor_scalar(
                qstage[:, nq % 2, 0:ncol], p[:, 0:ncol], bq_pp[:, fc:fc + 1], None,
                op0=ALU.add), [("qev", 1)])
            b.wait("sync", "qev")
            b.op("sync", lambda sy, fc=fc, s0=s0, ncol=ncol, nq=nq: sy.dma_start(
                out=qT_ov[:, fc, s0:s0 + ncol], in_=qstage[:, nq % 2, 0:ncol]),
                [(f"qst{nq % 2}", 16)])
            nq += 1

    pso2 = [ps_c, ps_d]
    nkv = 0
    for i in range(NT):
        for kv in range(2):
            for (f0, ncol) in n_chunks(D):
                p = pso2[nkv % 2]
                if nkv >= 2:
                    b.wait("tensor", "kvev", nkv - 1)
                for cd in range(8):
                    b.op("tensor", lambda t, p=p, i=i, cd=cd, kv=kv, f0=f0, ncol=ncol:
                         t.matmul(p[:, 0:ncol], xnT[:, cd, 128 * i:128 * (i + 1)],
                                  wq[:, cd, D + kv * D + f0:D + kv * D + f0 + ncol],
                                  start=(cd == 0), stop=(cd == 7)),
                         [("kvmm", 1)] if cd == 7 else [])
                b.wait("vector", "kvmm")
                if nkv >= 2:
                    b.wait("vector", f"kvst{nkv % 2}", 16 * (nkv // 2))
                b.op("vector", lambda v, p=p, kv=kv, f0=f0, ncol=ncol, nkv=nkv:
                     v.scalar_tensor_tensor(
                         kvstage[:, nkv % 2, 0:ncol], p[:, 0:ncol], 1.0,
                         bkv_bc[:, kv * D + f0:kv * D + f0 + ncol],
                         op0=ALU.mult, op1=ALU.add), [("kvev", 1)])
                b.wait("sync", "kvev")
                b.op("sync", lambda sy, i=i, kv=kv, f0=f0, ncol=ncol, nkv=nkv:
                     sy.dma_start(
                         out=(k_ov if kv == 0 else v_ov)[:, i, f0:f0 + ncol],
                         in_=kvstage[:, nkv % 2, 0:ncol]), [(f"kvst{nkv % 2}", 16)])
                nkv += 1

    b.wait("sync", "xnst")
    b.wait("sync", "qst0")
    b.wait("sync", "qst1")
    b.wait("sync", "kvst0")
    b.wait("sync", "kvst1")

    b.finalize(stack)
    nc._stack = stack
    return nc



KB = 1024
NKB = L // KB
QBW = 256


def build_launch2():
    import contextlib
    nc = bass.Bass("TRN2", target_bir_lowering=False, debug=False)
    qT = nc.dram_tensor("qT", [D, T], BF16, kind="ExternalInput").ap()
    k_all = nc.dram_tensor("k_all", [L, D], BF16, kind="ExternalInput").ap()
    v_all = nc.dram_tensor("v_all", [L, D], BF16, kind="ExternalInput").ap()
    kT_own = nc.dram_tensor("kT_own", [D, T], BF16, kind="ExternalInput").ap()
    v_own = nc.dram_tensor("v_own", [T, D], BF16, kind="ExternalInput").ap()
    gamma = nc.dram_tensor("gamma", [1, 1], F32, kind="ExternalInput").ap()
    wproj = nc.dram_tensor("wproj", [D, D], F32, kind="ExternalInput").ap()
    bproj = nc.dram_tensor("bproj", [D], F32, kind="ExternalInput").ap()
    attn_o = nc.dram_tensor("attn_o", [T, D], F32, kind="ExternalOutput").ap()
    lng_sc = nc.dram_tensor("lng_sc", [1], F32)

    attn_ov = attn_o.rearrange("(t p) d -> p t d", p=128)
    kT_own_v = kT_own.rearrange("(c p) t -> p c t", p=128)
    v_own_v = v_own.rearrange("(t p) d -> p t d", p=128)
    qT_v = qT.rearrange("(c p) t -> p c t", p=128)

    stack = contextlib.ExitStack()
    sb = lambda *a: stack.enter_context(nc.sbuf_tensor(*a))
    psal = lambda *a: stack.enter_context(nc.psum_tensor(*a))

    qTs = sb("qTs", [128, 8, T], BF16)
    O_sb = sb("O_sb", [128, NT, D], F32)
    Z_sb = sb("Z_sb", [128, NT], F32)
    ktb = sb("ktb", [128, 8, KB], BF16)
    vb = sb("vb", [128, 8, D], BF16)
    P_sb = sb("P_sb", [128, 2, QBW], BF16)
    onecol = sb("onecol", [128, 1], BF16)
    wp = sb("wp", [128, 8, D], BF16)
    bp_bc = sb("bp_bc", [128, D], F32)
    diff = sb("diff", [128, 128], F32)
    diffc = sb("diffc", [128, 128], F32)
    gemask = sb("gemask", [128, 128], F32)
    M_sb = sb("M_sb", [128, 128], F32)
    SM_sb = sb("SM_sb", [128, 128], F32)
    Pc_sb = sb("Pc_sb", [128, 2, 128], BF16)
    lngcol = sb("lngcol", [128, 1], F32)
    gcol = sb("gcol", [1, 2], F32)
    ktj = sb("ktj", [128, 8, 128], BF16)
    vtj = sb("vtj", [128, D], BF16)
    reccol = sb("reccol", [128, 1], F32)
    anorm = sb("anorm", [128, D], BF16)
    aT = sb("aT", [128, 8, 128], BF16)
    ostage = sb("ostage", [128, 2, D], F32)
    ident = sb("ident", [128, 128], BF16)

    ps_O0a = psal("ps_O0a", [128, 512], F32)
    ps_O0b = psal("ps_O0b", [128, 512], F32)
    ps_O1a = psal("ps_O1a", [128, 512], F32)
    ps_O1b = psal("ps_O1b", [128, 512], F32)
    ps_st = psal("ps_st", [128, QBW], F32)
    ps_rs0 = psal("ps_rs0", [128, 512], F32)
    ps_rs1 = psal("ps_rs1", [128, 512], F32)
    ps_t = psal("ps_t", [128, 128], BF16)

    b = B(nc)

    b.op("vector", lambda v: v.memset(onecol[:, :], 1.0), [("init", 1)])
    b.op("vector", lambda v: v.memset(O_sb[:, :, :], 0.0), [("init", 1)])
    b.op("vector", lambda v: v.memset(Z_sb[:, :], 0.0), [("init", 1)])
    b.op("gpsimd", lambda g: g.memset(ident[:, :], 1.0), [("gp", 1)])
    b.wait("gpsimd", "gp")
    b.op("gpsimd", lambda g: g.affine_select(ident[:, :], ident[:, :],
                                             pattern=[[-1, 128]],
                                             compare_op=ALU.is_equal, fill=0.0,
                                             base=0, channel_multiplier=1),
         [("init", 1)])
    b.op("gpsimd", lambda g: g.iota(diff[:, :], pattern=[[-1, 128]], base=0,
                                    channel_multiplier=1,
                                    allow_small_or_imprecise_dtypes=True),
         [("init", 1)])
    b.wait("vector", "init")
    b.op("vector", lambda v: v.tensor_scalar(diffc[:, :], diff[:, :], 0.0, None,
                                             op0=ALU.max), [("init", 1)])
    b.op("vector", lambda v: v.tensor_scalar(gemask[:, :], diff[:, :], 0.0, None,
                                             op0=ALU.is_ge), [("init", 1)])
    b.op("sync", lambda sy: sy.dma_start(out=gcol[0:1, 0:1], in_=gamma[:, :]),
         [("gl", 16)])
    b.wait("scalar", "gl")
    b.op("scalar", lambda s: s.activation(gcol[0:1, 1:2], gcol[0:1, 0:1], AF.Ln),
         [("gln", 1)])
    b.wait("sync", "gln")
    b.op("sync", lambda sy: sy.dma_start(out=lng_sc.rearrange("(o d) -> o d", o=1),
                                         in_=gcol[0:1, 1:2]), [("gl", 16)])
    b.wait("sync", "gl")
    b.op("sync", lambda sy: sy.dma_start(
        out=lngcol[:, :],
        in_=lng_sc.rearrange("(o d) -> o d", o=1).to_broadcast([128, 1])),
        [("gl", 16)])
    b.wait("vector", "gl")
    b.wait("vector", "init")
    b.op("vector", lambda v: v.tensor_scalar(M_sb[:, :], diffc[:, :], lngcol[:, 0:1],
                                             None, op0=ALU.mult), [("m1", 1)])
    b.wait("scalar", "m1")
    b.op("scalar", lambda s: s.activation(SM_sb[:, :], M_sb[:, :], AF.Exp), [("m2", 1)])
    b.wait("vector", "m2")
    b.op("vector", lambda v: v.tensor_tensor(M_sb[:, :], SM_sb[:, :], gemask[:, :],
                                             op=ALU.mult), [("m3", 1)])

    for c in range(8):
        b.op("sync", lambda sy, c=c: sy.dma_start(out=qTs[:, c, :], in_=qT_v[:, c, :]),
             [("ql", 16)])
    for c in range(8):
        b.op("gpsimd", lambda g, c=c: g.dma_start(out=wp[:, c, :],
                                                  in_=wproj[128 * c:128 * (c + 1), :]),
             [("wl", 16)])
    b.op("sync", lambda sy: sy.dma_start(
        out=bp_bc[:, :],
        in_=bproj.rearrange("(o d) -> o d", o=1).to_broadcast([128, D])), [("wl", 16)])

    b.wait("tensor", "ql")
    b.wait("tensor", "init")

    O_ps = [(ps_O0a, ps_O0b), (ps_O1a, ps_O1b)]
    rs_ps = [ps_rs0, ps_rs1]
    qblocks = [(i * QBW, min(QBW, T - i * QBW)) for i in range(ceil_div(T, QBW))]
    n_exp = 0
    n_oev = 0
    for kb in range(NKB):
        if kb > 0:
            b.wait("sync", "oev", n_oev)
        for c in range(8):
            b.op("sync", lambda sy, kb=kb, c=c: sy.dma_start_transpose(
                ktb[:, c, :], k_all[KB * kb:KB * (kb + 1), 128 * c:128 * (c + 1)]),
                [("kvl", 16)])
        for t8 in range(8):
            b.op("sync", lambda sy, kb=kb, t8=t8: sy.dma_start(
                out=vb[:, t8, :],
                in_=v_all[KB * kb + 128 * t8:KB * kb + 128 * (t8 + 1), :]),
                [("kvl", 16)])
        b.wait("tensor", "kvl")
        for (q0, qw) in qblocks:
            nsub = ceil_div(qw, 128)
            for kt in range(8):
                if n_exp > 0:
                    b.wait("tensor", "exp", n_exp)
                for cd in range(8):
                    b.op("tensor", lambda t, cd=cd, kt=kt, q0=q0, qw=qw: t.matmul(
                        ps_st[:, 0:qw], ktb[:, cd, 128 * kt:128 * (kt + 1)],
                        qTs[:, cd, q0:q0 + qw], start=(cd == 0), stop=(cd == 7)),
                        [("stmm", 1)] if cd == 7 else [])
                b.wait("scalar", "stmm")
                if b.val("pfree") >= 2:
                    b.wait("scalar", "pfree", b.val("pfree") - 1)
                b.op("scalar", lambda s, kt=kt, qw=qw: s.activation(
                    P_sb[:, kt % 2, 0:qw], ps_st[:, 0:qw], AF.Exp, scale=1.0 / 32.0),
                    [("exp", 1)])
                n_exp += 1
                b.wait("tensor", "exp", n_exp)
                for s_ in range(nsub):
                    pa, pb_ = O_ps[s_]
                    b.op("tensor", lambda t, s_=s_, kt=kt, pa=pa: t.matmul(
                        pa[:, :], P_sb[:, kt % 2, 128 * s_:128 * (s_ + 1)],
                        vb[:, kt, 0:512], start=(kt == 0), stop=(kt == 7)), [])
                    b.op("tensor", lambda t, s_=s_, kt=kt, pb_=pb_: t.matmul(
                        pb_[:, :], P_sb[:, kt % 2, 128 * s_:128 * (s_ + 1)],
                        vb[:, kt, 512:1024], start=(kt == 0), stop=(kt == 7)), [])
                    b.op("tensor", lambda t, s_=s_, kt=kt: t.matmul(
                        rs_ps[s_][:, 0:1], P_sb[:, kt % 2, 128 * s_:128 * (s_ + 1)],
                        onecol[:, :], start=(kt == 0), stop=(kt == 7)),
                        [("pfree", 1)] if s_ == nsub - 1 else [])
            b.wait("vector", "pfree")
            for s_ in range(nsub):
                jt = (q0 // 128) + s_
                pa, pb_ = O_ps[s_]
                b.op("vector", lambda v, jt=jt, pa=pa: v.tensor_tensor(
                    O_sb[:, jt, 0:512], O_sb[:, jt, 0:512], pa[:, :], op=ALU.add), [])
                b.op("vector", lambda v, jt=jt, pb_=pb_: v.tensor_tensor(
                    O_sb[:, jt, 512:1024], O_sb[:, jt, 512:1024], pb_[:, :],
                    op=ALU.add), [])
                b.op("vector", lambda v, jt=jt, s_=s_: v.tensor_tensor(
                    Z_sb[:, jt:jt + 1], Z_sb[:, jt:jt + 1], rs_ps[s_][:, 0:1],
                    op=ALU.add), [("oev", 16 if False else 1)])
            n_oev = b.val("oev")
            b.wait("tensor", "oev", n_oev)

    b.wait("vector", "m3")
    b.wait("tensor", "m3")
    for j in range(NT):
        b.op("sync", lambda sy, j=j: sy.dma_start(out=ktj[:, :, :],
                                                  in_=kT_own_v[:, :, 128 * j:128 * (j + 1)]),
             [("cl", 16)])
        b.op("sync", lambda sy, j=j: sy.dma_start(out=vtj[:, :], in_=v_own_v[:, j, :]),
             [("cl", 16)])
        b.wait("tensor", "cl")
        for cd in range(8):
            b.op("tensor", lambda t, cd=cd, j=j: t.matmul(
                ps_st[:, 0:128], ktj[:, cd, :], qTs[:, cd, 128 * j:128 * (j + 1)],
                start=(cd == 0), stop=(cd == 7)), [("cmm", 1)] if cd == 7 else [])
        b.wait("vector", "cmm")
        b.op("vector", lambda v: v.tensor_tensor(SM_sb[:, :], ps_st[:, 0:128],
                                                 M_sb[:, :], op=ALU.mult), [("csm", 1)])
        b.wait("scalar", "csm")
        b.op("scalar", lambda s: s.activation(Pc_sb[:, 0, :], SM_sb[:, :], AF.Exp,
                                              scale=1.0 / 32.0), [("cexp", 1)])
        b.op("scalar", lambda s: s.activation(Pc_sb[:, 1, :], ps_st[:, 0:128], AF.Exp,
                                              scale=1.0 / 32.0), [("cexp", 1)])
        b.wait("vector", "cexp")
        b.op("vector", lambda v: v.tensor_tensor(P_sb[:, 0, 0:128], Pc_sb[:, 0, :],
                                                 Pc_sb[:, 1, :], op=ALU.subtract),
             [("cdp", 1)])
        b.wait("tensor", "cdp")
        b.op("tensor", lambda t: t.matmul(ps_O0a[:, :], P_sb[:, 0, 0:128],
                                          vtj[:, 0:512], start=True, stop=True), [])
        b.op("tensor", lambda t: t.matmul(ps_O0b[:, :], P_sb[:, 0, 0:128],
                                          vtj[:, 512:1024], start=True, stop=True), [])
        b.op("tensor", lambda t: t.matmul(ps_rs0[:, 0:1], P_sb[:, 0, 0:128],
                                          onecol[:, :], start=True, stop=True),
             [("comm", 1)])
        b.wait("vector", "comm")
        b.op("vector", lambda v, j=j: v.tensor_tensor(
            O_sb[:, j, 0:512], O_sb[:, j, 0:512], ps_O0a[:, :], op=ALU.add), [])
        b.op("vector", lambda v, j=j: v.tensor_tensor(
            O_sb[:, j, 512:1024], O_sb[:, j, 512:1024], ps_O0b[:, :], op=ALU.add), [])
        b.op("vector", lambda v, j=j: v.tensor_tensor(
            Z_sb[:, j:j + 1], Z_sb[:, j:j + 1], ps_rs0[:, 0:1], op=ALU.add),
            [("cov", 1)])
        b.wait("tensor", "cov")
        b.wait("sync", "cov")

    b.wait("tensor", "wl")
    b.wait("vector", "wl")
    nprj = 0
    for j in range(NT):
        b.wait("vector", "anm")
        b.op("vector", lambda v, j=j: v.reciprocal(reccol[:, :], Z_sb[:, j:j + 1]),
             [("rec", 1)])
        b.wait("vector", "rec")
        b.wait("vector", "atp")
        b.op("vector", lambda v, j=j: v.tensor_scalar(
            anorm[:, :], O_sb[:, j, :], reccol[:, 0:1], None, op0=ALU.mult),
            [("anm", 1)])
        b.wait("tensor", "anm")
        b.wait("vector", "pmm")
        for c in range(8):
            b.op("tensor", lambda t, c=c: t.transpose(
                ps_t[:, :], anorm[:, 128 * c:128 * (c + 1)], ident[:, :]), [("atp", 1)])
            b.wait("vector", "atp")
            b.op("vector", lambda v, c=c: v.tensor_copy(aT[:, c, :], ps_t[:, :]),
                 [("atc", 1)])
            b.wait("tensor", "atc")
        for (f0, fw) in n_chunks(D):
            puse = ps_O0a if (f0 == 0) else ps_O0b
            for cd in range(8):
                b.op("tensor", lambda t, cd=cd, f0=f0, fw=fw, puse=puse: t.matmul(
                    puse[:, 0:fw], aT[:, cd, :], wp[:, cd, f0:f0 + fw],
                    start=(cd == 0), stop=(cd == 7)), [("pmm", 1)] if cd == 7 else [])
        b.wait("vector", "pmm")
        b.op("vector", lambda v, j=j: v.scalar_tensor_tensor(
            ostage[:, j % 2, 0:512], ps_O0a[:, :], 1.0, bp_bc[:, 0:512],
            op0=ALU.mult, op1=ALU.add), [])
        b.op("vector", lambda v, j=j: v.scalar_tensor_tensor(
            ostage[:, j % 2, 512:1024], ps_O0b[:, :], 1.0, bp_bc[:, 512:1024],
            op0=ALU.mult, op1=ALU.add), [("pev", 1)])
        b.wait("sync", "pev")
        b.op("sync", lambda sy, j=j: sy.dma_start(out=attn_ov[:, j, :],
                                                  in_=ostage[:, j % 2, :]),
             [(f"post{j % 2}", 16)])
        nprj += 1
        if nprj >= 2:
            b.wait("vector", f"post{nprj % 2}", 16 * (nprj // 2))
        b.wait("tensor", "pev")

    b.wait("sync", "post0")
    b.wait("sync", "post1")
    b.finalize(stack)
    nc._stack = stack
    return nc



NE = 16
FF = 1408
NFC = FF // 128
CAP = 2560
NGT = L // 128
TG = 3
MOE_GROUPS = [(g * TG * 128, min(TG * 128, T - g * TG * 128))
              for g in range(ceil_div(NT, TG))]


def build_launch3():
    import contextlib
    nc = bass.Bass("TRN2", target_bir_lowering=False, debug=False)
    xnT_all = nc.dram_tensor("xnT_all", [D, L], BF16, kind="ExternalInput").ap()
    xnT_own = nc.dram_tensor("xnT_own", [D, T], BF16, kind="ExternalInput").ap()
    xn_own = nc.dram_tensor("xn_own", [T, D], BF16, kind="ExternalInput").ap()
    x_own = nc.dram_tensor("x_own", [T, D], F32, kind="ExternalInput").ap()
    attn_own = nc.dram_tensor("attn_own", [T, D], F32, kind="ExternalInput").ap()
    tile_sel = nc.dram_tensor("tile_sel", [NGT, NT], F32, kind="ExternalInput").ap()
    router_w = nc.dram_tensor("router_w", [D, NE], F32, kind="ExternalInput").ap()
    router_b = nc.dram_tensor("router_b", [NE], F32, kind="ExternalInput").ap()
    w_gate = nc.dram_tensor("w_gate", [NE * D, FF], F32, kind="ExternalInput").ap()
    w_up = nc.dram_tensor("w_up", [NE * D, FF], F32, kind="ExternalInput").ap()
    w_down = nc.dram_tensor("w_down", [NE * FF, D], F32, kind="ExternalInput").ap()
    sw_gate = nc.dram_tensor("sw_gate", [D, FF], F32, kind="ExternalInput").ap()
    sw_up = nc.dram_tensor("sw_up", [D, FF], F32, kind="ExternalInput").ap()
    sw_down = nc.dram_tensor("sw_down", [FF, D], F32, kind="ExternalInput").ap()
    x1_o = nc.dram_tensor("x1_o", [T, D], F32, kind="ExternalOutput").ap()

    x1_ov = x1_o.rearrange("(t p) d -> p t d", p=128)
    xn_ov = xn_own.rearrange("(t p) d -> p t d", p=128)
    x_ov = x_own.rearrange("(t p) d -> p t d", p=128)
    at_ov = attn_own.rearrange("(t p) d -> p t d", p=128)
    xnT_own_v = xnT_own.rearrange("(c p) t -> p c t", p=128)
    xnT_all_v = xnT_all.rearrange("(c p) t -> p c t", p=128)
    ts_v = tile_sel.rearrange("(t p) j -> p t j", p=128)

    stack = contextlib.ExitStack()
    sb = lambda *a: stack.enter_context(nc.sbuf_tensor(*a))
    psal = lambda *a: stack.enter_context(nc.psum_tensor(*a))

    xnTo = sb("xnTo", [128, 8, T], BF16)
    xnTa = sb("xnTa", [128, 2, 8, 128], BF16)
    rw_s = sb("rw_s", [128, 8, NE], BF16)
    rb_bc = sb("rb_bc", [128, NE], F32)
    iota16 = sb("iota16", [128, NE], F32)
    ustrict = sb("ustrict", [128, 128], BF16)
    identf = sb("identf", [128, 128], F32)
    onecol_bf = sb("onecol_bf", [128, 1], BF16)
    ones1f = sb("ones1f", [1, 128], F32)
    selcol = sb("selcol", [128, NT], F32)
    countsT = sb("countsT", [128, NGT], F32)
    ones_t = sb("ones_t", [128, NGT], F32)
    countsS = sb("countsS", [128, NGT], F32)
    inclT = sb("inclT", [128, 128], F32)
    lg = sb("lg", [128, NE], F32)
    expv = sb("expv", [128, NE], F32)
    probs = sb("probs", [128, NE], F32)
    mcols = sb("mcols", [128, 12], F32)
    mx8 = sb("mx8", [128, 8], F32)
    mi8 = sb("mi8", [128, 8], U32)
    eqs = sb("eqs", [128, 2, NE], F32)
    oh12 = sb("oh12", [128, NE], BF16)
    rowb = sb("rowb", [1, NE], F32)
    posw = sb("posw", [128, 2], F32)
    junk16 = sb("junk16", [128, 2 * NE], F32)
    gstore = sb("gstore", [128, NT, 4], F32)
    ge_col = sb("ge_col", [128, 3], F32)
    wgu = sb("wgu", [128, 8, 2 * FF], BF16)
    wdn = sb("wdn", [128, NFC, D], BF16)
    hg_sb = sb("hg_sb", [128, TG * 128], BF16)
    hu_sb = sb("hu_sb", [128, TG * 128], BF16)
    tt_sb = sb("tt_sb", [128, TG * 128], BF16)
    ffn_acc = sb("ffn_acc", [128, NT, D], BF16)
    cmb = sb("cmb", [128, 2, D], F32)
    cmb2 = sb("cmb2", [128, 2, D], F32)
    cmbn = sb("cmbn", [128, 2, D], BF16)

    ps_hg = psal("ps_hg", [128, TG * 128], F32)
    ps_hu = psal("ps_hu", [128, TG * 128], F32)
    ps_d0 = psal("ps_d0", [128, 512], F32)
    ps_d1 = psal("ps_d1", [128, 512], F32)
    ps_d2 = psal("ps_d2", [128, 512], F32)
    ps_d3 = psal("ps_d3", [128, 512], F32)
    ps_d4 = psal("ps_d4", [128, 512], F32)
    ps_d5 = psal("ps_d5", [128, 512], F32)

    b = B(nc)

    b.op("gpsimd", lambda g: g.iota(iota16[:, :], pattern=[[1, NE]], base=0,
                                    channel_multiplier=0,
                                    allow_small_or_imprecise_dtypes=True), [("cst", 1)])
    b.op("gpsimd", lambda g: g.memset(ustrict[:, :], 1.0), [("cst", 1)])
    b.wait("gpsimd", "cst")
    b.op("gpsimd", lambda g: g.affine_select(ustrict[:, :], ustrict[:, :],
                                             pattern=[[1, 128]],
                                             compare_op=ALU.is_gt, fill=0.0,
                                             base=0, channel_multiplier=-1),
         [("cst", 1)])
    b.op("gpsimd", lambda g: g.memset(identf[:, :], 1.0), [("cst", 1)])
    b.wait("gpsimd", "cst")
    b.op("gpsimd", lambda g: g.affine_select(identf[:, :], identf[:, :],
                                             pattern=[[-1, 128]],
                                             compare_op=ALU.is_equal, fill=0.0,
                                             base=0, channel_multiplier=1), [("cst", 1)])
    b.op("vector", lambda v: v.memset(onecol_bf[:, :], 1.0), [("cst", 1)])
    b.op("vector", lambda v: v.memset(ones1f[:, :], 1.0), [("cst", 1)])
    b.op("vector", lambda v: v.memset(countsT[:, :], 0.0), [("cst", 1)])
    b.op("vector", lambda v: v.memset(countsS[:, :], 0.0), [("cst", 1)])
    b.op("vector", lambda v: v.memset(ones_t[:, :], 1.0), [("cst", 1)])
    b.op("vector", lambda v: v.memset(ffn_acc[:, :, :], 0.0), [("cst", 1)])
    for c in range(8):
        b.op("gpsimd", lambda g, c=c: g.dma_start(
            out=rw_s[:, c, :], in_=router_w[128 * c:128 * (c + 1), :]), [("wl", 16)])
    b.op("sync", lambda sy: sy.dma_start(
        out=rb_bc[:, :],
        in_=router_b.rearrange("(o d) -> o d", o=1).to_broadcast([128, NE])),
        [("wl", 16)])
    b.op("sync", lambda sy: sy.dma_start(out=selcol[:, :], in_=ts_v[:, 0, :]),
         [("wl", 16)])
    for c in range(8):
        b.op("sync", lambda sy, c=c: sy.dma_start(out=xnTo[:, c, :],
                                                  in_=xnT_own_v[:, c, :]), [("wl", 16)])

    b.wait("tensor", "wl")
    b.wait("tensor", "cst")
    b.wait("vector", "wl")
    b.wait("scalar", "wl")

    def routing_tile(src_chunks, jtag, store_j=None):
        b.wait("tensor", "lgv")
        b.wait("tensor", "bs2")
        b.wait("tensor", "ps")
        b.wait("tensor", "ohm")
        for cd in range(8):
            b.op("tensor", lambda t, cd=cd: t.matmul(
                ps_hg[:, 0:NE], src_chunks(cd), rw_s[:, cd, :],
                start=(cd == 0), stop=(cd == 7)), [("lgm", 1)] if cd == 7 else [])
        b.wait("vector", "lgm")
        b.op("vector", lambda v: v.scalar_tensor_tensor(
            lg[:, :], ps_hg[:, 0:NE], 1.0, rb_bc[:, :], op0=ALU.mult, op1=ALU.add),
            [("lgv", 1)])
        b.wait("vector", "lgv")
        b.op("vector", lambda v: v.max(out=mx8[:, :], in_=lg[:, :]), [("mxv", 1)])
        b.wait("vector", "mxv")
        b.op("vector", lambda v: v.max_index(mi8[:, :], mx8[:, :], lg[:, :]),
             [("miv", 1)])
        b.wait("vector", "miv")
        b.op("vector", lambda v: v.tensor_copy(mcols[:, 3:4], mi8[:, 0:1]), [("i1", 1)])
        b.op("vector", lambda v: v.tensor_copy(mcols[:, 4:5], mi8[:, 1:2]), [("i1", 1)])
        b.wait("vector", "i1")
        b.op("vector", lambda v: v.tensor_scalar(eqs[:, 0, :], iota16[:, :],
                                                 mcols[:, 3:4], None,
                                                 op0=ALU.is_equal), [("eq", 1)])
        b.op("vector", lambda v: v.tensor_scalar(eqs[:, 1, :], iota16[:, :],
                                                 mcols[:, 4:5], None,
                                                 op0=ALU.is_equal), [("eq", 1)])
        b.wait("vector", "eq")
        b.op("vector", lambda v: v.tensor_tensor(oh12[:, :], eqs[:, 0, :],
                                                 eqs[:, 1, :], op=ALU.add), [("oh", 1)])
        if store_j is not None:
            j = store_j
            b.op("vector", lambda v: v.tensor_reduce(mcols[:, 0:1], lg[:, :],
                                                     axis=mybir.AxisListType.X,
                                                     op=ALU.max, negate=True),
                 [("sm", 1)])
            b.wait("scalar", "sm")
            b.op("scalar", lambda s: s.activation(expv[:, :], lg[:, :], AF.Exp,
                                                  bias=mcols[:, 0:1]), [("sm", 1)])
            b.wait("vector", "sm")
            b.op("vector", lambda v: v.tensor_reduce(mcols[:, 1:2], expv[:, :],
                                                     axis=mybir.AxisListType.X,
                                                     op=ALU.add), [("sm", 1)])
            b.wait("vector", "sm")
            b.op("vector", lambda v: v.reciprocal(mcols[:, 2:3], mcols[:, 1:2]),
                 [("sm", 1)])
            b.wait("vector", "sm")
            b.op("vector", lambda v: v.tensor_scalar(probs[:, :], expv[:, :],
                                                     mcols[:, 2:3], None,
                                                     op0=ALU.mult), [("sm", 1)])
            b.wait("vector", "sm")
            b.op("vector", lambda v: v.scalar_tensor_tensor(
                junk16[:, 0:NE], probs[:, :], 1.0, eqs[:, 0, :], op0=ALU.mult,
                op1=ALU.mult, accum_out=mcols[:, 7:8]), [("sm", 1)])
            b.op("vector", lambda v: v.scalar_tensor_tensor(
                junk16[:, NE:2 * NE], probs[:, :], 1.0, eqs[:, 1, :], op0=ALU.mult,
                op1=ALU.mult, accum_out=mcols[:, 8:9]), [("sm", 1)])
            b.wait("vector", "sm")
            b.op("vector", lambda v: v.tensor_tensor(mcols[:, 5:6], mcols[:, 7:8],
                                                     mcols[:, 8:9], op=ALU.add),
                 [("sm", 1)])
            b.wait("vector", "sm")
            b.op("vector", lambda v: v.tensor_scalar(mcols[:, 5:6], mcols[:, 5:6],
                                                     1e-9, None, op0=ALU.add),
                 [("sm", 1)])
            b.wait("vector", "sm")
            b.op("vector", lambda v: v.reciprocal(mcols[:, 6:7], mcols[:, 5:6]),
                 [("sm", 1)])
            b.wait("vector", "sm")
            b.wait("tensor", "oh")
            b.op("tensor", lambda t, j=j: t.matmul(
                ps_hg[0:1, 16:32], selcol[:, j:j + 1], inclT[:, 0:16],
                start=True, stop=True), [("bs1", 1)])
            b.wait("vector", "bs1")
            b.op("vector", lambda v: v.tensor_copy(rowb[0:1, :], ps_hg[0:1, 16:32]),
                 [("bs2", 1)])
            b.wait("tensor", "bs2")
            b.op("tensor", lambda t: t.matmul(ps_hu[:, 0:NE], ones1f[0:1, :],
                                              rowb[0:1, :], start=True, stop=False),
                 [])
            b.op("tensor", lambda t: t.matmul(ps_hu[:, 0:NE], ustrict[:, :],
                                              oh12[:, :], start=False, stop=True),
                 [("fm", 1)])
            b.wait("vector", "fm")
            b.op("vector", lambda v: v.scalar_tensor_tensor(
                junk16[:, 0:NE], ps_hu[:, 0:NE], 1.0, eqs[:, 0, :], op0=ALU.mult,
                op1=ALU.mult, accum_out=posw[:, 0:1]), [("ps", 1)])
            b.op("vector", lambda v: v.scalar_tensor_tensor(
                junk16[:, NE:2 * NE], ps_hu[:, 0:NE], 1.0, eqs[:, 1, :], op0=ALU.mult,
                op1=ALU.mult, accum_out=posw[:, 1:2]), [("ps", 1)])
            b.wait("vector", "ps")
            b.op("vector", lambda v: v.tensor_scalar(mcols[:, 9:10], posw[:, 0:1],
                                                     float(CAP), None,
                                                     op0=ALU.is_lt), [("gt", 1)])
            b.op("vector", lambda v: v.tensor_scalar(mcols[:, 10:11], posw[:, 1:2],
                                                     float(CAP), None,
                                                     op0=ALU.is_lt), [("gt", 1)])
            b.wait("vector", "gt")
            b.op("vector", lambda v: v.tensor_tensor(mcols[:, 7:8], mcols[:, 7:8],
                                                     mcols[:, 6:7], op=ALU.mult),
                 [("gt", 1)])
            b.op("vector", lambda v: v.tensor_tensor(mcols[:, 8:9], mcols[:, 8:9],
                                                     mcols[:, 6:7], op=ALU.mult),
                 [("gt", 1)])
            b.wait("vector", "gt")
            b.op("vector", lambda v, j=j: v.tensor_copy(gstore[:, j, 0:2],
                                                        mcols[:, 3:5]), [("gt", 1)])
            b.op("vector", lambda v, j=j: v.tensor_tensor(
                gstore[:, j, 2:3], mcols[:, 7:8], mcols[:, 9:10], op=ALU.mult),
                [("gt", 1)])
            b.op("vector", lambda v, j=j: v.tensor_tensor(
                gstore[:, j, 3:4], mcols[:, 8:9], mcols[:, 10:11], op=ALU.mult),
                [("gt", 1)])
            b.wait("vector", "gt")

    for gt in range(NGT):
        buf = gt % 2
        if gt >= 2:
            b.wait("sync", "ohm", gt - 1)
        b.op("sync", lambda sy, gt=gt, buf=buf: sy.dma_start(
            out=xnTa[:, buf, :, :], in_=xnT_all_v[:, :, 128 * gt:128 * (gt + 1)]),
            [(f"xal{buf}", 16)])
        b.wait("tensor", f"xal{buf}", 16 * (gt // 2 + 1))
        routing_tile(lambda cd, buf=buf: xnTa[:, buf, cd, :], "rt")
        b.wait("tensor", "oh")
        b.op("tensor", lambda t, gt=gt: t.matmul(ps_hu[0:16, 0:1], oh12[:, :],
                                                 onecol_bf[:, :], start=True,
                                                 stop=True), [("ohm", 1)])
        b.wait("vector", "ohm")
        b.op("vector", lambda v, gt=gt: v.tensor_copy(countsT[0:16, gt:gt + 1],
                                                      ps_hu[0:16, 0:1]), [("cnt", 1)])
        b.wait("tensor", "cnt")
    b.wait("vector", "cnt")
    b.op("vector", lambda v: v.tensor_tensor_scan(
        countsS[0:16, :], ones_t[0:16, :], countsT[0:16, :], op0=ALU.mult,
        op1=ALU.add, initial=0.0), [("scan", 1)])
    b.wait("tensor", "scan")
    b.op("tensor", lambda t: t.transpose(ps_hg[:, 0:128], countsS[:, :], identf[:, :]),
         [("tpc", 1)])
    b.wait("vector", "tpc")
    b.op("vector", lambda v: v.tensor_copy(inclT[:, :], ps_hg[:, 0:128]), [("tpi", 1)])
    b.wait("tensor", "tpi")
    b.wait("vector", "tpi")

    for j in range(NT):
        routing_tile(lambda cd, j=j: xnTo[:, cd, 128 * j:128 * (j + 1)], "rt",
                     store_j=j)

    b.wait("tensor", "ps")
    b.wait("tensor", "lgv")
    b.wait("tensor", "bs2")
    for e in range(NE + 1):
        if e > 0:
            b.wait("gpsimd", "dfc")
        for c in range(8):
            b.op("gpsimd", lambda g, e=e, c=c: g.dma_start(
                out=wgu[:, c, 0:FF],
                in_=(w_gate[D * e + 128 * c:D * e + 128 * (c + 1), :] if e < NE
                     else sw_gate[128 * c:128 * (c + 1), :])), [("wle", 16)])
            b.op("gpsimd", lambda g, e=e, c=c: g.dma_start(
                out=wgu[:, c, FF:2 * FF],
                in_=(w_up[D * e + 128 * c:D * e + 128 * (c + 1), :] if e < NE
                     else sw_up[128 * c:128 * (c + 1), :])), [("wle", 16)])
        for fc in range(NFC):
            b.op("gpsimd", lambda g, e=e, fc=fc: g.dma_start(
                out=wdn[:, fc, :],
                in_=(w_down[FF * e + 128 * fc:FF * e + 128 * (fc + 1), :] if e < NE
                     else sw_down[128 * fc:128 * (fc + 1), :])), [("wle", 16)])
        b.wait("tensor", "wle")
        for (t0, tw) in MOE_GROUPS:
            ntc = ceil_div(tw, 128)
            dps = [(ps_d0, ps_d1), (ps_d2, ps_d3), (ps_d4, ps_d5)]
            for fc in range(NFC):
                for cd in range(8):
                    b.op("tensor", lambda t, cd=cd, fc=fc, t0=t0, tw=tw: t.matmul(
                        ps_hg[:, 0:tw], wgu[:, cd, 128 * fc:128 * (fc + 1)],
                        xnTo[:, cd, t0:t0 + tw], start=(cd == 0), stop=(cd == 7)), [])
                    b.op("tensor", lambda t, cd=cd, fc=fc, t0=t0, tw=tw: t.matmul(
                        ps_hu[:, 0:tw], wgu[:, cd, FF + 128 * fc:FF + 128 * (fc + 1)],
                        xnTo[:, cd, t0:t0 + tw], start=(cd == 0), stop=(cd == 7)),
                        [("humm", 1)] if cd == 7 else [])
                b.wait("scalar", "humm")
                b.wait("scalar", "ttm")
                b.op("scalar", lambda s, tw=tw: s.activation(hg_sb[:, 0:tw],
                                                             ps_hg[:, 0:tw], AF.Silu),
                     [("sil", 1)])
                b.op("scalar", lambda s, tw=tw: s.activation(hu_sb[:, 0:tw],
                                                             ps_hu[:, 0:tw], AF.Copy),
                     [("sil", 1)])
                b.wait("vector", "sil")
                b.wait("vector", "dfc")
                b.op("vector", lambda v, tw=tw: v.tensor_tensor(
                    tt_sb[:, 0:tw], hg_sb[:, 0:tw], hu_sb[:, 0:tw], op=ALU.mult),
                    [("ttm", 1)])
                b.wait("tensor", "ttm")
                b.wait("tensor", "sil")
                for tc in range(ntc):
                    pa, pb_ = dps[tc]
                    b.op("tensor", lambda t, fc=fc, tc=tc, pa=pa: t.matmul(
                        pa[:, :], tt_sb[:, 128 * tc:128 * (tc + 1)],
                        wdn[:, fc, 0:512], start=(fc == 0), stop=(fc == NFC - 1)), [])
                    b.op("tensor", lambda t, fc=fc, tc=tc, pb_=pb_: t.matmul(
                        pb_[:, :], tt_sb[:, 128 * tc:128 * (tc + 1)],
                        wdn[:, fc, 512:1024], start=(fc == 0), stop=(fc == NFC - 1)),
                        [("dfc", 1)] if tc == ntc - 1 else [])
            b.wait("vector", "dfc")
            for tc in range(ntc):
                j = t0 // 128 + tc
                pa, pb_ = dps[tc]
                if e < NE:
                    b.op("vector", lambda v, j=j, e=e: v.tensor_scalar(
                        ge_col[:, 0:1], gstore[:, j, 0:1], float(e), None,
                        op0=ALU.is_equal), [("ge", 1)])
                    b.op("vector", lambda v, j=j, e=e: v.tensor_scalar(
                        ge_col[:, 1:2], gstore[:, j, 1:2], float(e), None,
                        op0=ALU.is_equal), [("ge", 1)])
                    b.wait("vector", "ge")
                    b.op("vector", lambda v, j=j: v.tensor_tensor(
                        ge_col[:, 0:1], ge_col[:, 0:1], gstore[:, j, 2:3],
                        op=ALU.mult), [("ge", 1)])
                    b.op("vector", lambda v, j=j: v.tensor_tensor(
                        ge_col[:, 1:2], ge_col[:, 1:2], gstore[:, j, 3:4],
                        op=ALU.mult), [("ge", 1)])
                    b.wait("vector", "ge")
                    b.op("vector", lambda v: v.tensor_tensor(
                        ge_col[:, 2:3], ge_col[:, 0:1], ge_col[:, 1:2], op=ALU.add),
                        [("ge", 1)])
                    b.wait("vector", "ge")
                    gsc = ge_col[:, 2:3]
                else:
                    gsc = None
                for half, pp in ((0, pa), (1, pb_)):
                    if gsc is not None:
                        b.op("vector", lambda v, j=j, half=half, pp=pp, gsc=gsc:
                             v.scalar_tensor_tensor(
                                 ffn_acc[:, j, 512 * half:512 * (half + 1)], pp[:, :],
                                 gsc, ffn_acc[:, j, 512 * half:512 * (half + 1)],
                                 op0=ALU.mult, op1=ALU.add), [("fev", 1)])
                    else:
                        b.op("vector", lambda v, j=j, half=half, pp=pp:
                             v.scalar_tensor_tensor(
                                 ffn_acc[:, j, 512 * half:512 * (half + 1)], pp[:, :],
                                 1.0, ffn_acc[:, j, 512 * half:512 * (half + 1)],
                                 op0=ALU.mult, op1=ALU.add), [("fev", 1)])
            b.wait("tensor", "fev")

    for j in range(NT):
        par = j % 2
        if j >= 2:
            b.wait("sync", "cv", 3 * (j - 1))
            b.wait("sync", f"x1st{par}", 16 * (j // 2))
        b.op("sync", lambda sy, j=j, par=par: sy.dma_start(out=cmb[:, par, :],
                                                           in_=x_ov[:, j, :]),
             [(f"ld{par}", 16)])
        b.op("sync", lambda sy, j=j, par=par: sy.dma_start(out=cmb2[:, par, :],
                                                           in_=at_ov[:, j, :]),
             [(f"ld{par}", 16)])
        b.op("sync", lambda sy, j=j, par=par: sy.dma_start(out=cmbn[:, par, :],
                                                           in_=xn_ov[:, j, :]),
             [(f"ld{par}", 16)])
        b.wait("vector", f"ld{par}", 48 * (j // 2 + 1))
        b.op("vector", lambda v, j=j, par=par: v.tensor_tensor(
            cmb[:, par, :], cmb[:, par, :], cmb2[:, par, :], op=ALU.add), [("cv", 1)])
        b.wait("vector", "cv")
        b.op("vector", lambda v, j=j, par=par: v.tensor_tensor(
            cmb[:, par, :], cmb[:, par, :], cmbn[:, par, :], op=ALU.add), [("cv", 1)])
        b.wait("vector", "cv")
        b.wait("vector", "fev")
        b.op("vector", lambda v, j=j, par=par: v.tensor_tensor(
            cmb[:, par, :], cmb[:, par, :], ffn_acc[:, j, :], op=ALU.add), [("cv", 1)])
        b.wait("sync", "cv")
        b.op("sync", lambda sy, j=j, par=par: sy.dma_start(out=x1_ov[:, j, :],
                                                           in_=cmb[:, par, :]),
             [(f"x1st{par}", 16)])
    b.wait("sync", "x1st0")
    b.wait("sync", "x1st1")

    b.finalize(stack)
    nc._stack = stack
    return nc



INNER = 2048
KCONV = 4
KDW = 3


def build_launch4a():
    import contextlib
    nc = bass.Bass("TRN2", target_bir_lowering=False, debug=False)
    x1 = nc.dram_tensor("x1", [T, D], F32, kind="ExternalInput").ap()
    hm = nc.dram_tensor("hm", [128, 1], F32, kind="ExternalInput").ap()
    w_in = nc.dram_tensor("w_in", [D, 2 * INNER], F32, kind="ExternalInput").ap()
    conv_w = nc.dram_tensor("conv_w", [INNER, KCONV], F32, kind="ExternalInput").ap()
    conv_b = nc.dram_tensor("conv_b", [INNER], F32, kind="ExternalInput").ap()
    w_out = nc.dram_tensor("w_out", [INNER, D], F32, kind="ExternalInput").ap()
    dw_k = nc.dram_tensor("dw_k", [D, KDW], F32, kind="ExternalInput").ap()
    dw_b = nc.dram_tensor("dw_b", [D], F32, kind="ExternalInput").ap()
    pw_wT = nc.dram_tensor("pw_wT", [D, D], F32, kind="ExternalInput").ap()
    pw_b = nc.dram_tensor("pw_b", [D], F32, kind="ExternalInput").ap()
    attn1_o = nc.dram_tensor("attn1_o", [T, D], BF16, kind="ExternalOutput").ap()
    x1f_o = nc.dram_tensor("x1f_o", [T, D], BF16, kind="ExternalOutput").ap()

    attn1_ov = attn1_o.rearrange("(t p) d -> p t d", p=128)
    x1f_ov = x1f_o.rearrange("(t p) d -> p t d", p=128)

    stack = contextlib.ExitStack()
    sb = lambda *a: stack.enter_context(nc.sbuf_tensor(*a))
    psal = lambda *a: stack.enter_context(nc.psum_tensor(*a))

    xt = sb("xt", [128, 2, D], F32)
    xn1 = sb("xn1", [128, D], BF16)
    xn1T = sb("xn1T", [128, 8, T], BF16)
    hmc = sb("hmc", [128, 1], F32)
    st6 = sb("st6", [128, 12], F32)
    st2 = sb("st2", [128, 2], F32)
    rcol = sb("rcol", [128, 2], F32)
    epscol = sb("epscol", [128, 1], F32)
    ident = sb("ident", [128, 128], BF16)
    wbuf = sb("wbuf", [128, 8, 1024], BF16)
    wobuf = sb("wobuf", [128, 4, 1024], BF16)
    cwcol = sb("cwcol", [128, 16, KCONV], F32)
    cbcol = sb("cbcol", [128, 16], F32)
    dwcol = sb("dwcol", [128, 8, KDW], F32)
    dbcol = sb("dbcol", [128, 8], F32)
    pb_bc = sb("pb_bc", [128, D], F32)
    hTg = sb("hTg", [128, 4, T], BF16)
    sigg = sb("sigg", [128, 512], F32)
    ybuf = sb("ybuf", [128, T], F32)
    yT = sb("yT", [128, 4, T], BF16)
    a1acc = sb("a1acc", [128, NT, D], BF16)
    xcT = sb("xcT", [128, 8, T], BF16)
    stg = sb("stg", [128, 2, D], F32)
    stg2 = sb("stg2", [128, 2, D], BF16)
    ps_a = psal("ps_a", [128, 512], F32)
    ps_b = psal("ps_b", [128, 512], F32)
    ps_c = psal("ps_c", [128, 512], F32)
    ps_d = psal("ps_d", [128, 512], F32)
    ps_t = psal("ps_t", [128, 128], BF16)

    b = B(nc)

    b.op("vector", lambda v: v.memset(epscol[:, :], EPS_LN), [("cst", 1)])
    b.op("vector", lambda v: v.memset(a1acc[:, :, :], 0.0), [("cst", 1)])
    b.op("vector", lambda v: v.memset(ybuf[:, :], 0.0), [("cst", 1)])
    b.op("gpsimd", lambda g: g.memset(ident[:, :], 1.0), [("gp", 1)])
    b.wait("gpsimd", "gp")
    b.op("gpsimd", lambda g: g.affine_select(ident[:, :], ident[:, :],
                                             pattern=[[-1, 128]],
                                             compare_op=ALU.is_equal, fill=0.0,
                                             base=0, channel_multiplier=1), [("cst", 1)])
    b.op("sync", lambda sy: sy.dma_start(out=hmc[:, :], in_=hm[:, :]), [("wl", 16)])
    for c in range(16):
        b.op("sync", lambda sy, c=c: sy.dma_start(
            out=cwcol[:, c, :], in_=conv_w[128 * c:128 * (c + 1), :]), [("wl", 16)])
        b.op("sync", lambda sy, c=c: sy.dma_start(
            out=cbcol[:, c:c + 1],
            in_=conv_b[128 * c:128 * (c + 1)].rearrange("(p o) -> p o", o=1)),
            [("wl", 16)])
    for c in range(8):
        b.op("sync", lambda sy, c=c: sy.dma_start(
            out=dwcol[:, c, :], in_=dw_k[128 * c:128 * (c + 1), :]), [("wl", 16)])
        b.op("sync", lambda sy, c=c: sy.dma_start(
            out=dbcol[:, c:c + 1],
            in_=dw_b[128 * c:128 * (c + 1)].rearrange("(p o) -> p o", o=1)),
            [("wl", 16)])
    b.op("sync", lambda sy: sy.dma_start(
        out=pb_bc[:, :],
        in_=pw_b.rearrange("(o d) -> o d", o=1).to_broadcast([128, D])), [("wl", 16)])

    ln_done = []
    for i in range(NT):
        if i >= 2:
            b.wait("sync", "ln@vector", ln_done[i - 2])
        b.op("sync", lambda sy, i=i: sy.dma_start(out=xt[:, i % 2, :],
                                                  in_=x1[128 * i:128 * (i + 1), :]),
             [(f"xl{i % 2}", 16)])
        b.wait("vector", f"xl{i % 2}", 16 * (i // 2 + 1))
        if i == 0:
            b.wait("vector", "wl")
        b.wait("vector", "tpc")
        emit_ln_tile(b, xt[:, i % 2, :], xn1[:, :], st6, st2, rcol, "ln",
                     epscol=epscol[:, 0:1])
        if i == 0:
            b.wait("vector", "ln")
            b.op("vector", lambda v: v.tensor_scalar(xn1[:, :], xn1[:, :],
                                                     hmc[:, 0:1], None,
                                                     op0=ALU.mult), [("ln", 1)])
        ln_done.append(b.counts.get("ln@vector", 0))
        b.wait("tensor", "ln")
        b.wait("tensor", "cst")
        for c in range(8):
            b.op("tensor", lambda t, i=i, c=c: t.transpose(
                ps_t[:, :], xn1[:, 128 * c:128 * (c + 1)], ident[:, :]), [("tp", 1)])
            b.wait("vector", "tp")
            b.op("vector", lambda v, i=i, c=c: v.tensor_copy(
                xn1T[:, c, 128 * i:128 * (i + 1)], ps_t[:, :]), [("tpc", 1)])
            b.wait("tensor", "tpc")

    tchunks = n_chunks(T)
    for fg in range(4):
        b.wait("gpsimd", "wmm")
        for c in range(8):
            b.op("gpsimd", lambda g, fg=fg, c=c: g.dma_start(
                out=wbuf[:, c, 0:512],
                in_=w_in[128 * c:128 * (c + 1), 512 * fg:512 * (fg + 1)]),
                [("wle", 16)])
            b.op("gpsimd", lambda g, fg=fg, c=c: g.dma_start(
                out=wbuf[:, c, 512:1024],
                in_=w_in[128 * c:128 * (c + 1),
                         INNER + 512 * fg:INNER + 512 * (fg + 1)]), [("wle", 16)])
        b.wait("tensor", "wle")
        for fc in range(4):
            for (t0, tw) in tchunks:
                b.wait("tensor", "glu")
                for cd in range(8):
                    b.op("tensor", lambda t, cd=cd, fc=fc, t0=t0, tw=tw: t.matmul(
                        ps_a[:, 0:tw], wbuf[:, cd, 128 * fc:128 * (fc + 1)],
                        xn1T[:, cd, t0:t0 + tw], start=(cd == 0), stop=(cd == 7)), [])
                    b.op("tensor", lambda t, cd=cd, fc=fc, t0=t0, tw=tw: t.matmul(
                        ps_b[:, 0:tw], wbuf[:, cd, 512 + 128 * fc:512 + 128 * (fc + 1)],
                        xn1T[:, cd, t0:t0 + tw], start=(cd == 0), stop=(cd == 7)),
                        [("agmm", 1)] if cd == 7 else [])
                b.wait("scalar", "agmm")
                b.op("scalar", lambda s, tw=tw: s.activation(
                    sigg[:, 0:tw], ps_b[:, 0:tw], AF.Sigmoid), [("sig", 1)])
                b.wait("vector", "sig")
                b.op("vector", lambda v, fc=fc, t0=t0, tw=tw: v.tensor_tensor(
                    hTg[:, fc, t0:t0 + tw], ps_a[:, 0:tw], sigg[:, 0:tw],
                    op=ALU.mult), [("glu", 1)])
        for fc in range(4):
            ch = 4 * fg + fc
            b.wait("vector", "glu")
            b.wait("vector", "ysil")
            b.op("vector", lambda v, fc=fc, ch=ch: v.tensor_scalar(
                ybuf[:, 3:T], hTg[:, fc, 0:T - 3], cwcol[:, ch, 0:1], None,
                op0=ALU.mult), [("cv", 1)])
            for j in range(1, KCONV):
                b.wait("vector", "cv")
                b.op("vector", lambda v, fc=fc, ch=ch, j=j: v.scalar_tensor_tensor(
                    ybuf[:, 3:T], hTg[:, fc, j:T - 3 + j], cwcol[:, ch, j:j + 1],
                    ybuf[:, 3:T], op0=ALU.mult, op1=ALU.add), [("cv", 1)])
            b.wait("scalar", "cv")
            b.op("scalar", lambda s, fc=fc, ch=ch: s.activation(
                yT[:, fc, :], ybuf[:, :], AF.Silu, bias=cbcol[:, ch:ch + 1]),
                [("ysil", 1)])
        for c in range(4):
            b.op("gpsimd", lambda g, fg=fg, c=c: g.dma_start(
                out=wobuf[:, c, :],
                in_=w_out[512 * fg + 128 * c:512 * fg + 128 * (c + 1), :]),
                [("wo", 16)])
        b.wait("tensor", "wo")
        b.wait("tensor", "ysil")
        for jt in range(NT):
            for half in range(2):
                pp = ps_c if half == 0 else ps_d
                for fc in range(4):
                    b.op("tensor", lambda t, jt=jt, half=half, fc=fc, pp=pp: t.matmul(
                        pp[:, :], yT[:, fc, 128 * jt:128 * (jt + 1)],
                        wobuf[:, fc, 512 * half:512 * (half + 1)],
                        start=(fc == 0), stop=(fc == 3)),
                        [("wmm", 1)] if fc == 3 else [])
            b.wait("vector", "wmm")
            for half in range(2):
                pp = ps_c if half == 0 else ps_d
                b.op("vector", lambda v, jt=jt, half=half, pp=pp: v.tensor_tensor(
                    a1acc[:, jt, 512 * half:512 * (half + 1)],
                    a1acc[:, jt, 512 * half:512 * (half + 1)], pp[:, :], op=ALU.add),
                    [("wout", 1)])
            b.wait("tensor", "wout")

    b.wait("sync", "wout")
    for jt in range(NT):
        b.op("sync", lambda sy, jt=jt: sy.dma_start(out=attn1_ov[:, jt, :],
                                                    in_=a1acc[:, jt, :]),
             [("a1st", 16)])

    b.wait("vector", "ysil")
    for c in range(8):
        b.op("vector", lambda v, c=c: v.tensor_scalar(
            ybuf[:, 2:T], xn1T[:, c, 0:T - 2], dwcol[:, c, 0:1], None,
            op0=ALU.mult), [("dcv", 1)])
        for j in range(1, KDW):
            b.wait("vector", "dcv")
            b.op("vector", lambda v, c=c, j=j: v.scalar_tensor_tensor(
                ybuf[:, 2:T], xn1T[:, c, j:T - 2 + j], dwcol[:, c, j:j + 1],
                ybuf[:, 2:T], op0=ALU.mult, op1=ALU.add), [("dcv", 1)])
        b.wait("vector", "dcv")
        b.op("vector", lambda v, c=c: v.tensor_scalar(
            xcT[:, c, :], ybuf[:, :], dbcol[:, c:c + 1], None, op0=ALU.add),
            [("dcv", 1)])
        b.wait("vector", "dcv")

    b.wait("gpsimd", "wmm")
    for c in range(8):
        b.op("gpsimd", lambda g, c=c: g.dma_start(
            out=wbuf[:, c, :], in_=pw_wT[128 * c:128 * (c + 1), :]), [("pwl", 16)])
    b.wait("tensor", "pwl")
    b.wait("tensor", "dcv")
    for jt in range(NT):
        for half in range(2):
            pp = ps_c if half == 0 else ps_d
            b.wait("tensor", "pws")
            for cd in range(8):
                b.op("tensor", lambda t, jt=jt, half=half, cd=cd, pp=pp: t.matmul(
                    pp[:, :], xcT[:, cd, 128 * jt:128 * (jt + 1)],
                    wbuf[:, cd, 512 * half:512 * (half + 1)],
                    start=(cd == 0), stop=(cd == 7)),
                    [("pmm", 1)] if cd == 7 else [])
            b.wait("vector", "pmm")
            b.op("vector", lambda v, half=half, pp=pp, jt=jt: v.scalar_tensor_tensor(
                stg[:, 0, 512 * half:512 * (half + 1)], pp[:, :], 1.0,
                pb_bc[:, 512 * half:512 * (half + 1)], op0=ALU.mult, op1=ALU.add),
                [("pws", 1)])
        b.wait("scalar", "pws")
        b.op("scalar", lambda s, jt=jt: s.activation(stg[:, 1, :], stg[:, 0, :],
                                                     AF.Silu), [("psil", 1)])
        b.wait("tensor", "psil")
        b.wait("tensor", "xf")
        for c in range(8):
            b.op("tensor", lambda t, jt=jt, c=c: t.transpose(
                ps_t[:, :], xn1T[:, c, 128 * jt:128 * (jt + 1)], ident[:, :]),
                [("tp2", 1)])
            b.wait("vector", "tp2")
            b.op("vector", lambda v, jt=jt, c=c: v.tensor_copy(
                stg2[:, 0, 128 * c:128 * (c + 1)], ps_t[:, :]), [("tpc2", 1)])
            b.wait("tensor", "tpc2")
        b.wait("vector", "psil")
        b.wait("vector", "tpc2")
        b.op("vector", lambda v, jt=jt: v.tensor_tensor(
            stg2[:, 1, :], stg[:, 1, :], stg2[:, 0, :], op=ALU.add), [("xf", 1)])
        b.wait("sync", "xf")
        b.op("sync", lambda sy, jt=jt: sy.dma_start(out=x1f_ov[:, jt, :],
                                                    in_=stg2[:, 1, :]), [("xfst", 16)])
        b.wait("vector", "xfst")

    b.wait("sync", "a1st")
    b.wait("sync", "xfst")
    b.finalize(stack)
    nc._stack = stack
    return nc



NE1 = 8
HID = 256
EPS_RMS = 1e-6


def build_launch4b():
    import contextlib
    nc = bass.Bass("TRN2", target_bir_lowering=False, debug=False)
    x1 = nc.dram_tensor("x1", [T, D], F32, kind="ExternalInput").ap()
    attn1 = nc.dram_tensor("attn1", [T, D], BF16, kind="ExternalInput").ap()
    x1f = nc.dram_tensor("x1f", [T, D], BF16, kind="ExternalInput").ap()
    x1fT = nc.dram_tensor("x1fT", [D, T], BF16, kind="ExternalInput").ap()
    rw1T = nc.dram_tensor("rw1T", [D, NE1], F32, kind="ExternalInput").ap()
    rb1 = nc.dram_tensor("rb1", [NE1], F32, kind="ExternalInput").ap()
    ewT = nc.dram_tensor("ewT", [NE1 * D, D], F32, kind="ExternalInput").ap()
    eb = nc.dram_tensor("eb", [NE1, D], F32, kind="ExternalInput").ap()
    out_wT = nc.dram_tensor("out_wT", [D, D], F32, kind="ExternalInput").ap()
    out_b = nc.dram_tensor("out_b", [D], F32, kind="ExternalInput").ap()
    norm_w = nc.dram_tensor("norm_w", [D], F32, kind="ExternalInput").ap()
    fdwT = nc.dram_tensor("fdwT", [D, HID], F32, kind="ExternalInput").ap()
    fdb = nc.dram_tensor("fdb", [HID], F32, kind="ExternalInput").ap()
    fupT = nc.dram_tensor("fupT", [HID, D], F32, kind="ExternalInput").ap()
    fub = nc.dram_tensor("fub", [D], F32, kind="ExternalInput").ap()
    out_o = nc.dram_tensor("out_o", [T_SH, D], F32, kind="ExternalOutput").ap()
    comb_dr = nc.dram_tensor("comb_dr", [T, D], BF16)
    h_dr = nc.dram_tensor("h_dr", [T, D], BF16)

    out_ov = out_o.rearrange("(t p) d -> p t d", p=128)
    x1_ov = x1.rearrange("(t p) d -> p t d", p=128)
    at_ov = attn1.rearrange("(t p) d -> p t d", p=128)
    x1f_ov = x1f.rearrange("(t p) d -> p t d", p=128)
    x1fT_v = x1fT.rearrange("(c p) t -> p c t", p=128)
    comb_ov = comb_dr.rearrange("(t p) d -> p t d", p=128)
    h_ov = h_dr.rearrange("(t p) d -> p t d", p=128)

    stack = contextlib.ExitStack()
    sb = lambda *a: stack.enter_context(nc.sbuf_tensor(*a))
    psal = lambda *a: stack.enter_context(nc.psum_tensor(*a))

    xfT = sb("xfT", [128, 8, T], BF16)
    comb = sb("comb", [128, NT, D], BF16)
    wbuf = sb("wbuf", [128, 8, D], BF16)
    rw1s = sb("rw1s", [128, 8, NE1], BF16)
    rb1bc = sb("rb1bc", [128, NE1], F32)
    ebbc = sb("ebbc", [128, D], F32)
    obbc = sb("obbc", [128, D], F32)
    nwbc = sb("nwbc", [128, D], F32)
    fubbc = sb("fubbc", [128, D], F32)
    fdws = sb("fdws", [128, 8, 2 * 128], BF16)
    fdbcol = sb("fdbcol", [128, 2], F32)
    fups = sb("fups", [128, 2, D], BF16)
    lg8 = sb("lg8", [128, NE1], F32)
    mx8 = sb("mx8", [128, 8], F32)
    mi8 = sb("mi8", [128, 8], U32)
    mcols = sb("mcols", [128, 8], F32)
    g1store = sb("g1store", [128, NT, 4], F32)
    ge_col = sb("ge_col", [128, 3], F32)
    stg = sb("stg", [128, 3, D], F32)
    stgb = sb("stgb", [128, 2, D], BF16)
    ctb = sb("ctb", [128, 8, 128], BF16)
    htb = sb("htb", [128, 8, 512], BF16)
    gT = sb("gT", [128, 2, T], BF16)
    epsc = sb("epsc", [128, 1], F32)
    junk = sb("junk", [128, D], F32)
    ps_a = psal("ps_a", [128, 512], F32)
    ps_b = psal("ps_b", [128, 512], F32)
    ps_c = psal("ps_c", [128, 512], F32)
    ps_d = psal("ps_d", [128, 512], F32)

    b = B(nc)

    b.op("vector", lambda v: v.memset(epsc[:, :], EPS_RMS), [("cst", 1)])
    b.op("vector", lambda v: v.memset(comb[:, :, :], 0.0), [("cst", 1)])
    for c in range(8):
        b.op("sync", lambda sy, c=c: sy.dma_start(out=xfT[:, c, :],
                                                  in_=x1fT_v[:, c, :]), [("wl", 16)])
    for c in range(8):
        b.op("gpsimd", lambda g, c=c: g.dma_start(
            out=rw1s[:, c, :], in_=rw1T[128 * c:128 * (c + 1), :]), [("wl", 16)])
        b.op("gpsimd", lambda g, c=c: g.dma_start(
            out=fdws[:, c, :], in_=fdwT[128 * c:128 * (c + 1), :]), [("wl", 16)])
    for c in range(2):
        b.op("gpsimd", lambda g, c=c: g.dma_start(
            out=fups[:, c, :], in_=fupT[128 * c:128 * (c + 1), :]), [("wl", 16)])
        b.op("sync", lambda sy, c=c: sy.dma_start(
            out=fdbcol[:, c:c + 1],
            in_=fdb[128 * c:128 * (c + 1)].rearrange("(p o) -> p o", o=1)),
            [("wl", 16)])
    for (dst, srcv) in ((rb1bc, rb1), (obbc, out_b), (nwbc, norm_w), (fubbc, fub)):
        b.op("sync", lambda sy, dst=dst, srcv=srcv: sy.dma_start(
            out=dst[:, :],
            in_=srcv.rearrange("(o d) -> o d", o=1).to_broadcast(
                [128, dst.shape[-1]])), [("wl", 16)])

    b.wait("tensor", "wl")
    b.wait("vector", "wl")
    b.wait("scalar", "wl")

    for j in range(NT):
        b.wait("tensor", "lgv")
        for cd in range(8):
            b.op("tensor", lambda t, j=j, cd=cd: t.matmul(
                ps_a[:, 0:NE1], xfT[:, cd, 128 * j:128 * (j + 1)], rw1s[:, cd, :],
                start=(cd == 0), stop=(cd == 7)), [("lgm", 1)] if cd == 7 else [])
        b.wait("vector", "lgm")
        b.op("vector", lambda v: v.scalar_tensor_tensor(
            lg8[:, :], ps_a[:, 0:NE1], 1.0, rb1bc[:, :], op0=ALU.mult, op1=ALU.add),
            [("lgv", 1)])
        b.wait("vector", "lgv")
        b.op("vector", lambda v: v.max(out=mx8[:, :], in_=lg8[:, :]), [("mxv", 1)])
        b.wait("vector", "mxv")
        b.op("vector", lambda v: v.max_index(mi8[:, :], mx8[:, :], lg8[:, :]),
             [("miv", 1)])
        b.wait("vector", "miv")
        b.op("vector", lambda v, j=j: v.tensor_copy(g1store[:, j, 0:1], mi8[:, 0:1]),
             [("gs", 1)])
        b.op("vector", lambda v, j=j: v.tensor_copy(g1store[:, j, 1:2], mi8[:, 1:2]),
             [("gs", 1)])
        b.op("vector", lambda v: v.tensor_tensor(mcols[:, 0:1], mx8[:, 0:1],
                                                 mx8[:, 1:2], op=ALU.subtract),
             [("gs", 1)])
        b.wait("scalar", "gs")
        b.op("scalar", lambda s: s.activation(mcols[:, 1:2], mcols[:, 0:1],
                                              AF.Sigmoid), [("sg", 1)])
        b.wait("vector", "sg")
        b.op("vector", lambda v, j=j: v.tensor_copy(g1store[:, j, 2:3],
                                                    mcols[:, 1:2]), [("gs", 1)])
        b.op("vector", lambda v, j=j: v.tensor_scalar(
            g1store[:, j, 3:4], mcols[:, 1:2], -1.0, 1.0, op0=ALU.mult, op1=ALU.add),
            [("gs", 1)])
        b.wait("vector", "gs")

    b.wait("tensor", "lgv")
    for e in range(NE1):
        b.wait("gpsimd", "emm")
        for c in range(8):
            b.op("gpsimd", lambda g, e=e, c=c: g.dma_start(
                out=wbuf[:, c, :], in_=ewT[D * e + 128 * c:D * e + 128 * (c + 1), :]),
                [("ewl", 16)])
        b.wait("sync", "eb1")
        b.op("sync", lambda sy, e=e: sy.dma_start(
            out=ebbc[:, :],
            in_=eb[e:e + 1, :].to_broadcast([128, D])), [("ebl", 16)])
        b.wait("tensor", "ewl")
        for jt in range(NT):
            b.wait("tensor", "eev")
            for half, pp in ((0, ps_a), (1, ps_b)):
                for cd in range(8):
                    b.op("tensor", lambda t, jt=jt, cd=cd, half=half, pp=pp: t.matmul(
                        pp[:, :], xfT[:, cd, 128 * jt:128 * (jt + 1)],
                        wbuf[:, cd, 512 * half:512 * (half + 1)],
                        start=(cd == 0), stop=(cd == 7)),
                        [("emm", 1)] if (cd == 7 and half == 1) else [])
            b.wait("vector", "emm")
            b.wait("vector", "ebl")
            b.op("vector", lambda v, jt=jt, e=e: v.tensor_scalar(
                ge_col[:, 0:1], g1store[:, jt, 0:1], float(e), None,
                op0=ALU.is_equal), [("ge", 1)])
            b.op("vector", lambda v, jt=jt, e=e: v.tensor_scalar(
                ge_col[:, 1:2], g1store[:, jt, 1:2], float(e), None,
                op0=ALU.is_equal), [("ge", 1)])
            b.wait("vector", "ge")
            b.op("vector", lambda v, jt=jt: v.tensor_tensor(
                ge_col[:, 0:1], ge_col[:, 0:1], g1store[:, jt, 2:3], op=ALU.mult),
                [("ge", 1)])
            b.op("vector", lambda v, jt=jt: v.tensor_tensor(
                ge_col[:, 1:2], ge_col[:, 1:2], g1store[:, jt, 3:4], op=ALU.mult),
                [("ge", 1)])
            b.wait("vector", "ge")
            b.op("vector", lambda v: v.tensor_tensor(
                ge_col[:, 2:3], ge_col[:, 0:1], ge_col[:, 1:2], op=ALU.add),
                [("ge", 1)])
            for half, pp in ((0, ps_a), (1, ps_b)):
                b.op("vector", lambda v, half=half, pp=pp: v.scalar_tensor_tensor(
                    stg[:, half, 0:512], pp[:, :], 1.0,
                    ebbc[:, 512 * half:512 * (half + 1)], op0=ALU.mult, op1=ALU.add),
                    [("eb1", 1)])
            b.wait("scalar", "eb1")
            b.op("scalar", lambda s: s.activation(stgb[:, 0, 0:512], stg[:, 0, 0:512],
                                                  AF.Silu), [("esil", 1)])
            b.op("scalar", lambda s: s.activation(stgb[:, 0, 512:1024],
                                                  stg[:, 1, 0:512], AF.Silu),
                 [("esil", 1)])
            b.wait("vector", "esil")
            b.wait("vector", "ge")
            b.op("vector", lambda v, jt=jt: v.scalar_tensor_tensor(
                comb[:, jt, :], stgb[:, 0, :], ge_col[:, 2:3], comb[:, jt, :],
                op0=ALU.mult, op1=ALU.add), [("eev", 1)])
            b.wait("scalar", "eev")

    b.wait("sync", "eev")
    for jt in range(NT):
        b.op("sync", lambda sy, jt=jt: sy.dma_start(out=comb_ov[:, jt, :],
                                                    in_=comb[:, jt, :]),
             [("cst2", 16)])
    b.wait("gpsimd", "emm")
    for c in range(8):
        b.op("gpsimd", lambda g, c=c: g.dma_start(
            out=wbuf[:, c, :], in_=out_wT[128 * c:128 * (c + 1), :]), [("owl", 16)])
    b.wait("sync", "cst2")
    b.wait("scalar", "cst2")
    b.wait("tensor", "owl")

    for jt in range(NT):
        b.wait("sync", "hst")
        b.wait("scalar", "hst")
        for c in range(8):
            b.op("scalar", lambda s, jt=jt, c=c: s.dma_start_transpose(
                ctb[:, c, :], comb_dr[128 * jt:128 * (jt + 1),
                                      128 * c:128 * (c + 1)]), [("ctl", 16)])
        b.op("sync", lambda sy, jt=jt: sy.dma_start(out=stgb[:, 1, :],
                                                    in_=x1f_ov[:, jt, :]),
             [("xfl", 16)])
        b.wait("tensor", "ctl")
        b.wait("tensor", "mev")
        for half, pp in ((0, ps_c), (1, ps_d)):
            for cd in range(8):
                b.op("tensor", lambda t, jt=jt, cd=cd, half=half, pp=pp: t.matmul(
                    pp[:, :], ctb[:, cd, :], wbuf[:, cd, 512 * half:512 * (half + 1)],
                    start=(cd == 0), stop=(cd == 7)),
                    [("mmm", 1)] if (cd == 7 and half == 1) else [])
        b.wait("vector", "mmm")
        b.wait("vector", "xfl")
        for half, pp in ((0, ps_c), (1, ps_d)):
            b.op("vector", lambda v, half=half, pp=pp: v.scalar_tensor_tensor(
                stg[:, 2, 512 * half:512 * (half + 1)], pp[:, :], 1.0,
                obbc[:, 512 * half:512 * (half + 1)], op0=ALU.mult, op1=ALU.add),
                [("mev", 1)])
        b.wait("vector", "mev")
        b.op("vector", lambda v, jt=jt: v.tensor_tensor(
            stg[:, 2, :], stg[:, 2, :], stgb[:, 1, :], op=ALU.add), [("mev", 1)])
        b.wait("vector", "mev")
        b.op("vector", lambda v: v.scalar_tensor_tensor(
            junk[:, :], stg[:, 2, :], 1.0, stg[:, 2, :], op0=ALU.bypass,
            op1=ALU.mult, accum_out=mcols[:, 2:3]), [("mev", 1)])
        b.wait("vector", "mev")
        b.op("vector", lambda v: v.tensor_scalar(mcols[:, 3:4], mcols[:, 2:3],
                                                 1.0 / D, None, op0=ALU.mult),
             [("mev", 1)])
        b.wait("scalar", "mev")
        b.op("scalar", lambda s: s.activation(mcols[:, 4:5], mcols[:, 3:4], AF.Ln,
                                              bias=epsc[:, 0:1]), [("rms", 1)])
        b.wait("scalar", "rms")
        b.op("scalar", lambda s: s.activation(mcols[:, 5:6], mcols[:, 4:5], AF.Exp,
                                              scale=-0.5), [("rms", 1)])
        b.wait("vector", "rms")
        b.op("vector", lambda v: v.tensor_scalar(stg[:, 2, :], stg[:, 2, :],
                                                 mcols[:, 5:6], None, op0=ALU.mult),
             [("mev", 1)])
        b.wait("vector", "mev")
        b.op("vector", lambda v, jt=jt: v.tensor_tensor(
            stgb[:, 1, :], stg[:, 2, :], nwbc[:, :], op=ALU.mult), [("mev", 1)])
        b.wait("sync", "mev")
        b.op("sync", lambda sy, jt=jt: sy.dma_start(out=h_ov[:, jt, :],
                                                    in_=stgb[:, 1, :]), [("hst", 16)])

    b.wait("sync", "hst")
    b.wait("scalar", "hst")
    b.wait("tensor", "eb1")
    for (t0, tw) in n_chunks(T):
        b.wait("scalar", "gev")
        for c in range(8):
            b.op("scalar", lambda s, t0=t0, tw=tw, c=c: s.dma_start_transpose(
                htb[:, c, 0:tw], h_dr[t0:t0 + tw, 128 * c:128 * (c + 1)]),
                [("htl", 16)])
        b.wait("tensor", "htl")
        b.wait("tensor", "gev")
        for f2 in range(2):
            pp = ps_a if f2 == 0 else ps_b
            for cd in range(8):
                b.op("tensor", lambda t, f2=f2, cd=cd, t0=t0, tw=tw, pp=pp: t.matmul(
                    pp[:, 0:tw], fdws[:, cd, 128 * f2:128 * (f2 + 1)],
                    htb[:, cd, 0:tw], start=(cd == 0), stop=(cd == 7)),
                    [("gmm", 1)] if cd == 7 else [])
        b.wait("scalar", "gmm")
        for f2 in range(2):
            pp = ps_a if f2 == 0 else ps_b
            b.op("scalar", lambda s, f2=f2, t0=t0, tw=tw, pp=pp: s.activation(
                gT[:, f2, t0:t0 + tw], pp[:, 0:tw], AF.Gelu,
                bias=fdbcol[:, f2:f2 + 1]), [("gev", 1)])
    b.wait("tensor", "gev")

    b.wait("tensor", "mev")
    for jt in range(1, NT):
        b.wait("tensor", "fin")
        for half, pp in ((0, ps_c), (1, ps_d)):
            for f2 in range(2):
                b.op("tensor", lambda t, jt=jt, f2=f2, half=half, pp=pp: t.matmul(
                    pp[:, :], gT[:, f2, 128 * jt:128 * (jt + 1)],
                    fups[:, f2, 512 * half:512 * (half + 1)],
                    start=(f2 == 0), stop=(f2 == 1)),
                    [("umm", 1)] if (f2 == 1 and half == 1) else [])
        b.op("sync", lambda sy, jt=jt: sy.dma_start(out=stg[:, 0, :],
                                                    in_=x1_ov[:, jt, :]),
             [(f"fl{jt % 2}", 16)])
        b.op("sync", lambda sy, jt=jt: sy.dma_start(out=stgb[:, 0, :],
                                                    in_=at_ov[:, jt, :]),
             [(f"fl{jt % 2}", 16)])
        b.op("sync", lambda sy, jt=jt: sy.dma_start(out=stgb[:, 1, :],
                                                    in_=h_ov[:, jt, :]),
             [(f"fl{jt % 2}", 16)])
        b.wait("vector", f"fl{jt % 2}", 48 * ((jt - 1) // 2 + 1))
        b.wait("vector", "umm")
        for half, pp in ((0, ps_c), (1, ps_d)):
            b.op("vector", lambda v, half=half, pp=pp: v.scalar_tensor_tensor(
                stg[:, 1, 512 * half:512 * (half + 1)], pp[:, :], 1.0,
                fubbc[:, 512 * half:512 * (half + 1)], op0=ALU.mult, op1=ALU.add),
                [("fin", 1)])
        b.wait("vector", "fin")
        b.op("vector", lambda v: v.tensor_tensor(stg[:, 1, :], stg[:, 1, :],
                                                 stgb[:, 1, :], op=ALU.add),
             [("fin", 1)])
        b.wait("vector", "fin")
        b.op("vector", lambda v: v.tensor_tensor(stg[:, 1, :], stg[:, 1, :],
                                                 stgb[:, 0, :], op=ALU.add),
             [("fin", 1)])
        b.wait("vector", "fin")
        b.op("vector", lambda v: v.tensor_tensor(stg[:, 1, :], stg[:, 1, :],
                                                 stg[:, 0, :], op=ALU.add),
             [("fin", 1)])
        b.wait("sync", "fin")
        b.op("sync", lambda sy, jt=jt: sy.dma_start(out=out_ov[:, jt - 1, :],
                                                    in_=stg[:, 1, :]),
             [("ost", 16)])
        b.wait("vector", "ost")
    b.wait("sync", "ost")

    b.finalize(stack)
    nc._stack = stack
    return nc



def _x_shards(x):
    shards = []
    for c in range(NCORES):
        lo = T_SH * c - HALO
        if lo < 0:
            shards.append(np.concatenate([np.zeros((HALO, D), x.dtype), x[0:T_SH]], 0))
        else:
            shards.append(np.ascontiguousarray(x[lo:T_SH * c + T_SH]))
    return shards


def kernel(**inputs):
    inp = {k: np.asarray(v) for k, v in inputs.items()}
    x = inp["L0_attn_x"].reshape(L, D)
    xsh = _x_shards(x)
    cores = list(range(NCORES))

    nc1 = build_launch1()
    maps1 = [{"x": xsh[c], "wqkv": inp["L0_attn_Wqkv"], "bqkv": inp["L0_attn_bqkv"]}
             for c in range(NCORES)]
    r1 = [{k: np.asarray(v) for k, v in m.items()}
          for m in run_bass_kernel_spmd(nc1, maps1, core_ids=cores).results]

    k_all = np.concatenate([r1[c]["k_o"][HALO:] for c in range(NCORES)], 0)
    v_all = np.concatenate([r1[c]["v_o"][HALO:] for c in range(NCORES)], 0)

    nc2 = build_launch2()
    gamma = inp["L0_attn_gamma"].reshape(1, 1).astype(np.float32)
    maps2 = [{
        "qT": np.ascontiguousarray(r1[c]["qT_o"]),
        "k_all": k_all, "v_all": v_all,
        "kT_own": np.ascontiguousarray(r1[c]["k_o"].T),
        "v_own": np.ascontiguousarray(r1[c]["v_o"]),
        "gamma": gamma,
        "wproj": inp["L0_attn_Wproj"], "bproj": inp["L0_attn_bproj"],
    } for c in range(NCORES)]
    r2 = [{k: np.asarray(v) for k, v in m.items()}
          for m in run_bass_kernel_spmd(nc2, maps2, core_ids=cores).results]

    xn_all = np.concatenate([r1[c]["xn_o"][HALO:] for c in range(NCORES)], 0)
    xnT_all = np.ascontiguousarray(xn_all.T)
    nc3 = build_launch3()
    maps3 = []
    for c in range(NCORES):
        tile_sel = np.zeros((NGT, NT), np.float32)
        for j in range(NT):
            gt = 16 * c + j - 2
            if 0 <= gt < NGT:
                tile_sel[gt, j] = 1.0
        maps3.append({
            "xnT_all": xnT_all,
            "xnT_own": np.ascontiguousarray(r1[c]["xn_o"].T),
            "xn_own": np.ascontiguousarray(r1[c]["xn_o"]),
            "x_own": xsh[c],
            "attn_own": r2[c]["attn_o"],
            "tile_sel": tile_sel,
            "router_w": inp["L0_ffn_router_w"], "router_b": inp["L0_ffn_router_b"],
            "w_gate": inp["L0_ffn_w_gate"].reshape(NE * D, FF),
            "w_up": inp["L0_ffn_w_up"].reshape(NE * D, FF),
            "w_down": inp["L0_ffn_w_down"].reshape(NE * FF, D),
            "sw_gate": inp["L0_ffn_sw_gate"], "sw_up": inp["L0_ffn_sw_up"],
            "sw_down": inp["L0_ffn_sw_down"],
        })
    r3 = [{k: np.asarray(v) for k, v in m.items()}
          for m in run_bass_kernel_spmd(nc3, maps3, core_ids=cores).results]

    nc4a = build_launch4a()
    maps4a = []
    for c in range(NCORES):
        hm = np.full((128, 1), 0.0 if c == 0 else 1.0, np.float32)
        maps4a.append({
            "x1": r3[c]["x1_o"], "hm": hm,
            "w_in": inp["L1_attn_W_in"],
            "conv_w": inp["L1_attn_conv_w"].reshape(INNER, KCONV),
            "conv_b": inp["L1_attn_conv_b"],
            "w_out": inp["L1_attn_W_out"],
            "dw_k": inp["L1_ffn_dw_kernel"].reshape(D, KDW),
            "dw_b": inp["L1_ffn_dw_bias"],
            "pw_wT": np.ascontiguousarray(inp["L1_ffn_pw_w"].T),
            "pw_b": inp["L1_ffn_pw_b"],
        })
    r4a = [{k: np.asarray(v) for k, v in m.items()}
           for m in run_bass_kernel_spmd(nc4a, maps4a, core_ids=cores).results]

    nc4b = build_launch4b()
    ew = inp["L1_ffn_expert_w"]
    ewT = np.ascontiguousarray(
        np.concatenate([ew[e].T for e in range(NE1)], 0))
    maps4b = []
    for c in range(NCORES):
        maps4b.append({
            "x1": r3[c]["x1_o"], "attn1": r4a[c]["attn1_o"],
            "x1f": r4a[c]["x1f_o"],
            "x1fT": np.ascontiguousarray(r4a[c]["x1f_o"].T),
            "rw1T": np.ascontiguousarray(inp["L1_ffn_router_w"].T),
            "rb1": inp["L1_ffn_router_b"],
            "ewT": ewT, "eb": inp["L1_ffn_expert_b"],
            "out_wT": np.ascontiguousarray(inp["L1_ffn_out_w"].T),
            "out_b": inp["L1_ffn_out_b"],
            "norm_w": inp["L1_ffn_norm_w"],
            "fdwT": np.ascontiguousarray(inp["L1_ffn_ff_down_w"].T),
            "fdb": inp["L1_ffn_ff_down_b"],
            "fupT": np.ascontiguousarray(inp["L1_ffn_ff_up_w"].T),
            "fub": inp["L1_ffn_ff_up_b"],
        })
    r4b = [{k: np.asarray(v) for k, v in m.items()}
           for m in run_bass_kernel_spmd(nc4b, maps4b, core_ids=cores).results]

    out = np.concatenate([np.asarray(r4b[c]["out_o"]) for c in range(NCORES)], 0)
    return out.reshape(1, L, D)
